# version 01 (final)
# baseline (speedup 1.0000x reference)
import contextlib
import numpy as np
import ml_dtypes
import concourse.bass as bass
import concourse.mybir as mybir
from concourse.alu_op_type import AluOpType as OP
from concourse.bass_utils import run_bass_kernel_spmd

F32 = mybir.dt.float32
BF16 = mybir.dt.bfloat16
AF = mybir.ActivationFunctionType
AX = mybir.AxisListType
BFNP = ml_dtypes.bfloat16

NCORE = 8
B = 2
S = 4096
D = 4096
TPC = 1024
NEG = -1.0e30


class Ser:

    def __init__(self):
        self.steps = []

    def op(self, eng, fn):
        self.steps.append((eng, 1, fn))

    def dma(self, out, in_, q="sync"):
        self.steps.append((q, 16, lambda e: e.dma_start(out=out, in_=in_)))

    def build(self, nc, sem):
        pres = []
        c = 0
        for (_, w, _f) in self.steps:
            pres.append(c)
            c += w
        total = c
        order = ["sync", "tensor", "vector", "scalar", "gpsimd"]
        per = {k: [] for k in order}
        for (st, pre) in zip(self.steps, pres):
            per[st[0]].append((st[1], st[2], pre))
        with nc.Block() as block:
            for en in order:
                items = per[en]
                if not items and en != "gpsimd":
                    continue

                def mk(items):
                    def body(eng):
                        for (w, fn, pre) in items:
                            if pre > 0:
                                eng.wait_ge(sem, pre)
                            fn(eng).then_inc(sem, w)
                        eng.wait_ge(sem, total)
                    return body

                getattr(block, en)(mk(items))


def _mm(ps, lhsT, rhs, start, stop):
    return lambda e: e.matmul(ps, lhsT, rhs, start=start, stop=stop)


class LBuilder:

    def __init__(self, name):
        self.nc = bass.Bass("TRN2", target_bir_lowering=False)
        self.stack = contextlib.ExitStack()
        self.ser = Ser()
        self.name = name

    def dram_in(self, name, shape, dt=F32):
        return self.nc.dram_tensor(name, list(shape), dt, kind="ExternalInput").ap()

    def dram_out(self, name, shape, dt=F32):
        return self.nc.dram_tensor(name, list(shape), dt, kind="ExternalOutput").ap()

    def sb(self, name, shape, dt=F32):
        return self.stack.enter_context(self.nc.sbuf_tensor(name, list(shape), dt))

    def pl(self, i):
        if not hasattr(self, "_pool"):
            self._pool = [self.stack.enter_context(self.nc.psum_tensor(f"pool{k}", [128, 512], F32)) for k in range(8)]
        return self._pool[i]

    def finish(self):
        sem = self.stack.enter_context(self.nc.semaphore())
        self.ser.build(self.nc, sem)
        self.stack.close()
        return self.nc

    def loadcast(self, stage_f32, bf_dst, hbm_ap):
        self.ser.dma(stage_f32[:, :], hbm_ap)
        self.ser.op("scalar", lambda e, a=bf_dst, b=stage_f32: e.activation(out=a, in_=b[:, :], func=AF.Copy))

    def consts(self, have_bf_ident=True):
        self.id_f = self.sb("id_f", (128, 128), F32)
        self.ser.dma(self.id_f[:, :], self.dram_in("ident_f32", (128, 128)))
        if have_bf_ident:
            self.id_b = self.sb("id_b", (128, 128), BF16)
            self.ser.dma(self.id_b[:, :], self.dram_in("ident_bf16", (128, 128), BF16))
        self.ones_col_b = self.sb("ones_col_b", (128, 1), BF16)
        self.ser.op("vector", lambda e: e.memset(self.ones_col_b[:, :], 1.0))
        self.ones_row_f = self.sb("ones_row_f", (1, 128), F32)
        self.ser.op("vector", lambda e: e.memset(self.ones_row_f[:, :], 1.0))
        vals = [1e-5, 1e-6, 1.0, -1.0, -0.5, 2.0, 1.0 / float(np.sqrt(128.0)),
                1.0 / 256.0, 1.0 / 16.0, 1.0 / 4096.0]
        self.cst = self.sb("cst", (128, len(vals)), F32)
        for j, v in enumerate(vals):
            self.ser.op("vector", lambda e, j=j, v=v: e.memset(self.cst[:, j:j + 1], float(v)))
            self.nc.const_aps.register(F32, float(v), self.cst[:, j:j + 1])

    def col_from_1d(self, name, hbm_ap_2d, rows, width):
        tmp = self.sb(name + "_t", (rows, width), F32)
        out = self.sb(name, (width, rows), F32)
        p = self.pl(7)
        self.ser.dma(tmp[:, :], hbm_ap_2d)
        self.ser.op("tensor", _mm(p[0:width, 0:rows], tmp[0:rows, 0:width], self.id_f[0:rows, 0:rows], True, True))
        self.ser.op("scalar", lambda e: e.activation(out=out[:, :], in_=p[0:width, 0:rows], func=AF.Copy))
        return out

    def ln_dmajor(self, y_bf_tiles, xres_hbm_tiles, out_hbm_tiles, T, eps=1e-5, ntiles=32, bufs=None):
        bufs = bufs or {}
        st = bufs.get("st") or self.sb("ln_stage", (128, T), F32)
        r32 = bufs.get("r32") or self.sb("ln_r", (128, T), F32)
        rbf = self.sb("ln_rbf", (128, T), BF16)
        sqb = self.sb("ln_sq", (128, T), BF16)
        nt2 = T // 512
        p1s = [self.pl(0), self.pl(1)]
        p2s = [self.pl(2), self.pl(3)]
        rows = self.sb("ln_rows", (1, 6 * T), F32)
        Dtot = float(ntiles * 128)
        for ot in range(ntiles):
            self.ser.dma(st[:, :], xres_hbm_tiles[ot])
            self.ser.op("vector", lambda e, o=ot: e.tensor_tensor(out=r32[:, :], in0=st[:, :], in1=y_bf_tiles[o], op=OP.add))
            self.ser.op("scalar", lambda e: e.activation(out=rbf[:, :], in_=r32[:, :], func=AF.Copy))
            self.ser.op("scalar", lambda e: e.activation(out=sqb[:, :], in_=r32[:, :], func=AF.Square))
            for tc in range(nt2):
                self.ser.op("tensor", _mm(p1s[tc][0:1, 0:512], self.ones_col_b[:, :], rbf[:, tc * 512:(tc + 1) * 512], ot == 0, ot == ntiles - 1))
                self.ser.op("tensor", _mm(p2s[tc][0:1, 0:512], self.ones_col_b[:, :], sqb[:, tc * 512:(tc + 1) * 512], ot == 0, ot == ntiles - 1))
        for tc in range(nt2):
            sl = slice(tc * 512, (tc + 1) * 512)
            self.ser.op("vector", lambda e, s=sl, tc=tc: e.tensor_single_scalar(out=rows[0:1, 0:T][:, s], in_=p1s[tc][0:1, 0:512], scalar=1.0 / Dtot, op=OP.mult))
            self.ser.op("vector", lambda e, s=sl, tc=tc: e.tensor_single_scalar(out=rows[0:1, T:2 * T][:, s], in_=p2s[tc][0:1, 0:512], scalar=1.0 / Dtot, op=OP.mult))
        m_row = rows[0:1, 0:T]
        s2_row = rows[0:1, T:2 * T]
        msq = rows[0:1, 2 * T:3 * T]
        var = rows[0:1, 3 * T:4 * T]
        rstd = rows[0:1, 4 * T:5 * T]
        self.ser.op("vector", lambda e: e.tensor_mul(msq, m_row, m_row))
        self.ser.op("vector", lambda e: e.tensor_sub(var, s2_row, msq))
        self.ser.op("scalar", lambda e: e.activation(out=rstd, in_=var, func=AF.Ln, bias=eps))
        self.ser.op("scalar", lambda e: e.activation(out=rstd, in_=rstd, func=AF.Exp, scale=-0.5))
        m_bc = bufs.get("m_bc") or self.sb("ln_mbc", (128, T), F32)
        r_bc = self.sb("ln_rbc", (128, T), F32)
        for tc in range(nt2):
            sl = slice(tc * 512, (tc + 1) * 512)
            self.ser.op("tensor", _mm(p1s[0][:, 0:512], self.ones_row_f[:, :], m_row[:, sl], True, True))
            self.ser.op("scalar", lambda e, s=sl: e.activation(out=m_bc[:, s], in_=p1s[0][:, 0:512], func=AF.Copy))
            self.ser.op("tensor", _mm(p2s[0][:, 0:512], self.ones_row_f[:, :], rstd[:, sl], True, True))
            self.ser.op("scalar", lambda e, s=sl: e.activation(out=r_bc[:, s], in_=p2s[0][:, 0:512], func=AF.Copy))
        for ot in range(ntiles):
            self.ser.dma(st[:, :], xres_hbm_tiles[ot])
            self.ser.op("vector", lambda e, o=ot: e.tensor_tensor(out=r32[:, :], in0=st[:, :], in1=y_bf_tiles[o], op=OP.add))
            self.ser.op("vector", lambda e: e.tensor_sub(r32[:, :], r32[:, :], m_bc[:, :]))
            self.ser.op("vector", lambda e: e.tensor_mul(r32[:, :], r32[:, :], r_bc[:, :]))
            self.ser.dma(out_hbm_tiles[ot], r32[:, :], q="gpsimd")


def build_l0_qkv():
    L = LBuilder("l0qkv")
    xin = L.dram_in("xT", (D, TPC))
    wqT = L.dram_in("wqT", (D, 4096))
    wkT = L.dram_in("wkT", (D, 1024))
    wvT = L.dram_in("wvT", (D, 1024))
    cosT = L.dram_in("cosT", (128, TPC))
    sinT = L.dram_in("sinT", (128, TPC))
    qk_out = L.dram_out("qk", (40, 128, TPC), BF16)
    v_out = L.dram_out("v", (8, TPC, 128), BF16)
    L.consts()
    cos_sb = L.sb("cos", (128, TPC), F32)
    sin_sb = L.sb("sin", (128, TPC), F32)
    L.ser.dma(cos_sb[:, :], cosT)
    L.ser.dma(sin_sb[:, :], sinT)
    stage = L.sb("stage", (128, TPC), F32)
    xbf = [L.sb(f"xbf{i}", (128, TPC), BF16) for i in range(32)]
    for i in range(32):
        L.loadcast(stage, xbf[i][:, :], xin[i * 128:(i + 1) * 128, :])
    wstage = L.sb("wstage", (128, 512), F32)
    wbf = L.sb("wbf", (128, 512), BF16)
    pss = [L.pl(i) for i in range(8)]
    qf = L.sb("qf", (128, 512), F32)
    rot = L.sb("rot", (128, 512), F32)
    t1 = L.sb("t1", (128, 512), F32)
    qbf = L.sb("qbf", (128, 512), BF16)
    for og in range(10):
        for i in range(32):
            if og < 8:
                src = wqT[i * 128:(i + 1) * 128, og * 512:(og + 1) * 512]
            else:
                src = wkT[i * 128:(i + 1) * 128, (og - 8) * 512:(og - 7) * 512]
            L.ser.dma(wstage[:, :], src)
            L.ser.op("scalar", lambda e: e.activation(out=wbf[:, :], in_=wstage[:, :], func=AF.Copy))
            for ot in range(4):
                for tc in range(2):
                    L.ser.op("tensor", _mm(pss[ot * 2 + tc][:, :], wbf[:, ot * 128:(ot + 1) * 128],
                                           xbf[i][:, tc * 512:(tc + 1) * 512], i == 0, i == 31))
        for ot in range(4):
            head = og * 4 + ot
            for tc in range(2):
                p = pss[ot * 2 + tc]
                csl = slice(tc * 512, (tc + 1) * 512)
                L.ser.op("scalar", lambda e, p=p: e.activation(out=qf[:, :], in_=p[:, :], func=AF.Copy))
                L.ser.op("scalar", lambda e: e.activation(out=rot[0:64, :], in_=qf[64:128, :], func=AF.Copy, scale=-1.0))
                L.ser.op("scalar", lambda e: e.activation(out=rot[64:128, :], in_=qf[0:64, :], func=AF.Copy))
                L.ser.op("vector", lambda e, s=csl: e.tensor_mul(t1[:, :], qf[:, :], cos_sb[:, s]))
                L.ser.op("vector", lambda e, s=csl: e.tensor_mul(rot[:, :], rot[:, :], sin_sb[:, s]))
                L.ser.op("vector", lambda e: e.tensor_add(qbf[:, :], t1[:, :], rot[:, :]))
                L.ser.dma(qk_out[head, :, tc * 512:(tc + 1) * 512], qbf[:, :], q="gpsimd")
    wv_bf = [L.sb(f"wvbf{i}", (128, 1024), BF16) for i in range(32)]
    wst2 = L.sb("wst2", (128, 1024), F32)
    for i in range(32):
        L.ser.dma(wst2[:, :], wvT[i * 128:(i + 1) * 128, :])
        L.ser.op("scalar", lambda e, i=i: e.activation(out=wv_bf[i][:, :], in_=wst2[:, :], func=AF.Copy))
    vsb = L.sb("vsb", (128, 1024), BF16)
    for tt in range(8):
        for dc in range(2):
            p = pss[dc]
            for i in range(32):
                L.ser.op("tensor", _mm(p[:, :], xbf[i][:, tt * 128:(tt + 1) * 128],
                                       wv_bf[i][:, dc * 512:(dc + 1) * 512], i == 0, i == 31))
            L.ser.op("scalar", lambda e, p=p, dc=dc: e.activation(out=vsb[:, dc * 512:(dc + 1) * 512], in_=p[:, :], func=AF.Copy))
        for kvh in range(8):
            L.ser.dma(v_out[kvh, tt * 128:(tt + 1) * 128, :], vsb[:, kvh * 128:(kvh + 1) * 128], q="gpsimd")
    return L.finish()


def build_l0_attn():
    L = LBuilder("l0attn")
    qt_in = L.dram_in("qt", (8, 128, S), BF16)
    kt_in = L.dram_in("kt", (2, 128, S), BF16)
    v_in = L.dram_in("v", (2, S, 128), BF16)
    mask_in = L.dram_in("mask", (128, 128))
    ao_out = L.dram_out("ao", (8, 128, S), BF16)
    L.consts()
    mask = L.sb("maskb", (128, 128), F32)
    L.ser.dma(mask[:, :], mask_in)
    qt = L.sb("qts", (128, S), BF16)
    kt = L.sb("kts", (128, S), BF16)
    vs = L.sb("vss", (128, S), BF16)
    lg = L.sb("lg", (128, S), F32)
    abf = L.sb("abf", (128, S), BF16)
    atT = L.sb("atT", (128, S), BF16)
    red = L.sb("red", (128, 4), F32)
    ps_s = L.pl(0)
    ps_t = L.pl(1)
    ps_av = L.pl(2)
    obf = L.sb("obf", (128, 128), BF16)
    scale = 1.0 / float(np.sqrt(128.0))
    for hh in range(8):
        kv = hh // 4
        L.ser.dma(qt[:, :], qt_in[hh])
        if hh % 4 == 0:
            L.ser.dma(kt[:, :], kt_in[kv])
            L.ser.dma(vs[:, :], v_in[kv].rearrange("(a p) d -> p a d", p=128))
        for qi in range(32):
            klen = (qi + 1) * 128
            nch = (klen + 511) // 512
            for kc in range(nch):
                w = min(512, klen - kc * 512)
                L.ser.op("tensor", _mm(ps_s[:, 0:w], qt[:, qi * 128:(qi + 1) * 128], kt[:, kc * 512:kc * 512 + w], True, True))
                L.ser.op("scalar", lambda e, kc=kc, w=w: e.activation(out=lg[:, kc * 512:kc * 512 + w], in_=ps_s[:, 0:w], func=AF.Copy, scale=scale))
            L.ser.op("vector", lambda e, qi=qi: e.tensor_add(lg[:, qi * 128:(qi + 1) * 128], lg[:, qi * 128:(qi + 1) * 128], mask[:, :]))
            L.ser.op("vector", lambda e, klen=klen: e.reduce_max(red[:, 0:1], lg[:, 0:klen], axis=AX.X))
            L.ser.op("vector", lambda e: e.tensor_single_scalar(out=red[:, 1:2], in_=red[:, 0:1], scalar=-1.0, op=OP.mult))
            L.ser.op("scalar", lambda e, klen=klen: e.activation(out=lg[:, 0:klen], in_=lg[:, 0:klen], func=AF.Exp, bias=red[:, 1:2]))
            L.ser.op("vector", lambda e, klen=klen: e.reduce_sum(red[:, 2:3], lg[:, 0:klen], axis=AX.X))
            L.ser.op("vector", lambda e: e.reciprocal(red[:, 3:4], red[:, 2:3]))
            L.ser.op("vector", lambda e, klen=klen: e.tensor_scalar(out=abf[:, 0:klen], in0=lg[:, 0:klen], scalar1=red[:, 3:4], scalar2=None, op0=OP.mult))
            for tk in range(qi + 1):
                L.ser.op("tensor", _mm(ps_t[:, 0:128], abf[:, tk * 128:(tk + 1) * 128], L.id_b[:, :], True, True))
                L.ser.op("scalar", lambda e, tk=tk: e.activation(out=atT[:, tk * 128:(tk + 1) * 128], in_=ps_t[:, 0:128], func=AF.Copy))
            for tk in range(qi + 1):
                L.ser.op("tensor", _mm(ps_av[:, 0:128], vs[:, tk * 128:(tk + 1) * 128], atT[:, tk * 128:(tk + 1) * 128], tk == 0, tk == qi))
            L.ser.op("scalar", lambda e: e.activation(out=obf[:, :], in_=ps_av[:, 0:128], func=AF.Copy))
            L.ser.dma(ao_out[hh, :, qi * 128:(qi + 1) * 128], obf[:, :], q="gpsimd")
    return L.finish()


def build_l0_wo_ln():
    L = LBuilder("l0wo")
    ao_in = L.dram_in("ao", (D, TPC), BF16)
    woT = L.dram_in("woT", (D, D))
    xres = L.dram_in("xres", (D, TPC))
    x1_out = L.dram_out("x1", (D, TPC))
    L.consts()
    rhs = [L.sb(f"rhs{i}", (128, TPC), BF16) for i in range(32)]
    for i in range(32):
        L.ser.dma(rhs[i][:, :], ao_in[i * 128:(i + 1) * 128, :])
    wst = L.sb("wst", (128, 512), F32)
    wbf = L.sb("wbf", (128, 512), BF16)
    pss = [L.pl(i) for i in range(8)]
    ybf = [L.sb(f"ybf{o}", (128, TPC), BF16) for o in range(32)]
    for og in range(8):
        for i in range(32):
            L.ser.dma(wst[:, :], woT[i * 128:(i + 1) * 128, og * 512:(og + 1) * 512])
            L.ser.op("scalar", lambda e: e.activation(out=wbf[:, :], in_=wst[:, :], func=AF.Copy))
            for ot in range(4):
                for tc in range(2):
                    L.ser.op("tensor", _mm(pss[ot * 2 + tc][:, :], wbf[:, ot * 128:(ot + 1) * 128],
                                           rhs[i][:, tc * 512:(tc + 1) * 512], i == 0, i == 31))
        for ot in range(4):
            o = og * 4 + ot
            for tc in range(2):
                L.ser.op("scalar", lambda e, o=o, ot=ot, tc=tc: e.activation(
                    out=ybf[o][:, tc * 512:(tc + 1) * 512], in_=pss[ot * 2 + tc][:, :], func=AF.Copy))
    L.ln_dmajor(
        [ybf[o][:, :] for o in range(32)],
        [xres[o * 128:(o + 1) * 128, :] for o in range(32)],
        [x1_out[o * 128:(o + 1) * 128, :] for o in range(32)], TPC)
    return L.finish()


def build_ffn(mode):
    L = LBuilder("ffn_" + mode)
    xin = L.dram_in("xT", (D, TPC))
    wbT = L.dram_in("wbT", (D, D))
    wrT = L.dram_in("wrT", (D, 8))
    aT = L.dram_in("aT", (D, 128))
    bfl = L.dram_in("bfl", (128, D))
    bb = L.dram_in("bb", (32, 128))
    rep = L.dram_in("rep", (8, 128))
    br = L.dram_in("br", (1, 8))
    out_t = L.dram_out("out", (D, TPC))
    L.consts()
    stage = L.sb("stage", (128, TPC), F32)
    xbf = [L.sb(f"xbf{i}", (128, TPC), BF16) for i in range(32)]
    for i in range(32):
        L.loadcast(stage, xbf[i][:, :], xin[i * 128:(i + 1) * 128, :])
    bbs = L.sb("bbs", (32, 128), F32)
    L.ser.dma(bbs[:, :], bb)
    pbb = L.pl(6)
    bbc = L.sb("bbc", (128, 32), F32)
    L.ser.op("tensor", _mm(pbb[:, 0:32], bbs[0:32, :], L.id_f[0:32, 0:32], True, True))
    L.ser.op("scalar", lambda e: e.activation(out=bbc[:, :], in_=pbb[:, 0:32], func=AF.Copy))
    brs = L.sb("brs", (1, 8), F32)
    L.ser.dma(brs[:, :], br)
    brc = L.sb("brc", (8, 1), F32)
    pbr = L.pl(7)
    L.ser.op("tensor", _mm(pbr[0:8, 0:1], brs[0:1, 0:8], L.id_f[0:1, 0:1], True, True))
    L.ser.op("scalar", lambda e: e.activation(out=brc[:, :], in_=pbr[0:8, 0:1], func=AF.Copy))
    repb = L.sb("repb", (8, 128), BF16)
    L.ser.dma(stage[0:8, 0:128], rep)
    L.ser.op("scalar", lambda e: e.activation(out=repb[:, :], in_=stage[0:8, 0:128], func=AF.Copy))
    wrs = L.sb("wrs", (128, 8), F32)
    wrb = L.sb("wrb", (128, 8), BF16)
    ats = L.sb("ats", (128, 128), F32)
    atb = L.sb("atb", (128, 128), BF16)
    p_rs = [L.pl(0), L.pl(1)]
    p_hs = [L.pl(2), L.pl(3)]
    lt = L.sb("lt", (8, TPC), F32)
    hT = L.sb("hT", (128, TPC), F32)
    for i in range(32):
        L.ser.dma(wrs[:, :], wrT[i * 128:(i + 1) * 128, :])
        L.ser.op("scalar", lambda e: e.activation(out=wrb[:, :], in_=wrs[:, :], func=AF.Copy))
        L.ser.dma(ats[:, :], aT[i * 128:(i + 1) * 128, :])
        L.ser.op("scalar", lambda e: e.activation(out=atb[:, :], in_=ats[:, :], func=AF.Copy))
        for tc in range(2):
            csl = slice(tc * 512, (tc + 1) * 512)
            L.ser.op("tensor", _mm(p_rs[tc][0:8, 0:512], wrb[:, :], xbf[i][:, csl], i == 0, i == 31))
            L.ser.op("tensor", _mm(p_hs[tc][:, :], atb[:, :], xbf[i][:, csl], i == 0, i == 31))
            if i == 31:
                L.ser.op("vector", lambda e, s=csl, tc=tc: e.tensor_scalar(out=lt[:, s], in0=p_rs[tc][0:8, 0:512], scalar1=brc[0:8, 0:1], scalar2=None, op0=OP.add))
                L.ser.op("scalar", lambda e, s=csl, tc=tc: e.activation(out=hT[:, s], in_=p_hs[tc][:, :], func=AF.Copy))
    ltok = L.sb("ltok", (128, 8), F32)
    sc = L.sb("sc", (128, 16), F32)
    wtok = L.sb("wtok", (128, 8), F32)
    wT = L.sb("wT", (8, TPC), F32)
    p_t = L.pl(4)
    for tt in range(8):
        tsl = slice(tt * 128, (tt + 1) * 128)
        L.ser.op("tensor", _mm(p_t[:, 0:8], lt[0:8, tsl], L.id_f[0:8, 0:8], True, True))
        L.ser.op("scalar", lambda e: e.activation(out=ltok[:, :], in_=p_t[:, 0:8], func=AF.Copy))
        if mode == "top2":
            m1 = sc[:, 0:1]; nm2 = sc[:, 1:2]; m2 = sc[:, 2:3]; sg = sc[:, 3:4]; s1m = sc[:, 4:5]
            eq1 = L.sb(f"eq1_{tt}", (128, 8), F32); eq2 = L.sb(f"eq2_{tt}", (128, 8), F32)
            l2 = L.sb(f"l2_{tt}", (128, 8), F32)
            L.ser.op("vector", lambda e: e.reduce_max(m1, ltok[:, :], axis=AX.X))
            L.ser.op("vector", lambda e, eq1=eq1: e.tensor_scalar(out=eq1[:, :], in0=ltok[:, :], scalar1=m1, scalar2=None, op0=OP.is_equal))
            L.ser.op("vector", lambda e, eq1=eq1, l2=l2: e.scalar_tensor_tensor(out=l2[:, :], in0=eq1[:, :], scalar=NEG, in1=ltok[:, :], op0=OP.mult, op1=OP.add))
            L.ser.op("vector", lambda e, l2=l2: e.reduce_max(m2, l2[:, :], axis=AX.X))
            L.ser.op("vector", lambda e: e.tensor_single_scalar(out=nm2, in_=m2, scalar=-1.0, op=OP.mult))
            L.ser.op("scalar", lambda e: e.activation(out=sg, in_=m1, func=AF.Sigmoid, bias=nm2))
            L.ser.op("vector", lambda e, l2=l2, eq2=eq2: e.tensor_scalar(out=eq2[:, :], in0=l2[:, :], scalar1=m2, scalar2=None, op0=OP.is_equal))
            L.ser.op("vector", lambda e, eq1=eq1: e.tensor_scalar(out=eq1[:, :], in0=eq1[:, :], scalar1=sg, scalar2=None, op0=OP.mult))
            L.ser.op("vector", lambda e: e.tensor_scalar(out=s1m, in0=sg, scalar1=-1.0, scalar2=1.0, op0=OP.mult, op1=OP.add))
            L.ser.op("vector", lambda e, eq2=eq2: e.tensor_scalar(out=eq2[:, :], in0=eq2[:, :], scalar1=s1m, scalar2=None, op0=OP.mult))
            L.ser.op("vector", lambda e, eq1=eq1, eq2=eq2: e.tensor_add(wtok[:, :], eq1[:, :], eq2[:, :]))
        else:
            zm = L.sb(f"zm_{tt}", (128, 8), F32)
            eqk = L.sb(f"eqk_{tt}", (128, 8), F32)
            cum = sc[:, 0:1]; mk = sc[:, 1:2]; t1c = sc[:, 2:3]; si = sc[:, 3:4]
            tsum = sc[:, 4:5]; kz = sc[:, 5:6]; tmp1 = sc[:, 6:7]; kzr = sc[:, 7:8]; tau = sc[:, 8:9]
            L.ser.op("vector", lambda e, zm=zm: e.tensor_copy(out=zm[:, :], in_=ltok[:, :]))
            L.ser.op("vector", lambda e: e.memset(sc[:, 0:9], 0.0))
            for k in range(1, 9):
                L.ser.op("vector", lambda e, zm=zm: e.reduce_max(mk, zm[:, :], axis=AX.X))
                L.ser.op("vector", lambda e: e.tensor_add(cum, cum, mk))
                L.ser.op("vector", lambda e, k=k: e.tensor_scalar(out=t1c, in0=mk, scalar1=float(k), scalar2=1.0, op0=OP.mult, op1=OP.add))
                L.ser.op("vector", lambda e: e.tensor_tensor(out=si, in0=t1c, in1=cum, op=OP.is_gt))
                L.ser.op("vector", lambda e: e.tensor_mul(tmp1, si, mk))
                L.ser.op("vector", lambda e: e.tensor_add(tsum, tsum, tmp1))
                L.ser.op("vector", lambda e: e.tensor_add(kz, kz, si))
                if k < 8:
                    L.ser.op("vector", lambda e, zm=zm, eqk=eqk: e.tensor_scalar(out=eqk[:, :], in0=zm[:, :], scalar1=mk, scalar2=None, op0=OP.is_equal))
                    L.ser.op("vector", lambda e, zm=zm, eqk=eqk: e.scalar_tensor_tensor(out=zm[:, :], in0=eqk[:, :], scalar=NEG, in1=zm[:, :], op0=OP.mult, op1=OP.add))
            L.ser.op("vector", lambda e: e.tensor_single_scalar(out=tmp1, in_=tsum, scalar=1.0, op=OP.subtract))
            L.ser.op("vector", lambda e: e.reciprocal(kzr, kz))
            L.ser.op("vector", lambda e: e.tensor_mul(tau, tmp1, kzr))
            L.ser.op("vector", lambda e: e.tensor_scalar(out=wtok[:, :], in0=ltok[:, :], scalar1=tau, scalar2=0.0, op0=OP.subtract, op1=OP.max))
        L.ser.op("tensor", _mm(p_t[0:8, 0:128], wtok[:, :], L.id_f[:, :], True, True))
        L.ser.op("scalar", lambda e, s=tsl: e.activation(out=wT[:, s], in_=p_t[0:8, 0:128], func=AF.Copy))
    wTb = L.sb("wTb", (8, TPC), BF16)
    L.ser.op("scalar", lambda e: e.activation(out=wTb[:, :], in_=wT[:, :], func=AF.Copy))
    wrep = L.sb("wrep", (128, TPC), F32)
    p_w = L.pl(5)
    hwb = L.sb("hwb", (128, TPC), BF16)
    for tc in range(2):
        csl = slice(tc * 512, (tc + 1) * 512)
        L.ser.op("tensor", _mm(p_w[:, :], repb[:, :], wTb[:, csl], True, True))
        L.ser.op("scalar", lambda e, s=csl: e.activation(out=wrep[:, s], in_=p_w[:, :], func=AF.Copy))
    L.ser.op("vector", lambda e: e.tensor_tensor(out=hwb[:, :], in0=hT[:, :], in1=wrep[:, :], op=OP.mult))
    bflb = L.sb("bflb", (128, D), BF16)
    for ch in range(8):
        L.ser.dma(stage[:, 0:512], bfl[:, ch * 512:(ch + 1) * 512])
        L.ser.op("scalar", lambda e, ch=ch: e.activation(out=bflb[:, ch * 512:(ch + 1) * 512], in_=stage[:, 0:512], func=AF.Copy))
    wst = L.sb("wst", (128, 512), F32)
    wbf = L.sb("wbf", (128, 512), BF16)
    pss = [L.pl(i) for i in range(8)]
    ybf = [L.sb(f"ybf{o}", (128, TPC), BF16) for o in range(32)]
    for og in range(8):
        for i in range(32):
            L.ser.dma(wst[:, :], wbT[i * 128:(i + 1) * 128, og * 512:(og + 1) * 512])
            L.ser.op("scalar", lambda e: e.activation(out=wbf[:, :], in_=wst[:, :], func=AF.Copy))
            for ot in range(4):
                for tc in range(2):
                    L.ser.op("tensor", _mm(pss[ot * 2 + tc][:, :], wbf[:, ot * 128:(ot + 1) * 128],
                                           xbf[i][:, tc * 512:(tc + 1) * 512], i == 0, False))
        for ot in range(4):
            o = og * 4 + ot
            for tc in range(2):
                L.ser.op("tensor", _mm(pss[ot * 2 + tc][:, :], bflb[:, o * 128:(o + 1) * 128],
                                       hwb[:, tc * 512:(tc + 1) * 512], False, True))
                L.ser.op("vector", lambda e, o=o, ot=ot, tc=tc: e.tensor_scalar(
                    out=ybf[o][:, tc * 512:(tc + 1) * 512], in0=pss[ot * 2 + tc][:, :], scalar1=bbc[:, o:o + 1], scalar2=None, op0=OP.add))
    L.ln_dmajor(
        [ybf[o][:, :] for o in range(32)],
        [xin[o * 128:(o + 1) * 128, :] for o in range(32)],
        [out_t[o * 128:(o + 1) * 128, :] for o in range(32)], TPC,
        bufs=dict(st=stage, r32=hT, m_bc=wrep))
    return L.finish()


def build_l1_inproj():
    L = LBuilder("inproj")
    x2 = L.dram_in("x2", (B, D, S))
    wx = L.dram_in("wx", (D, 1024))
    wz = L.dram_in("wz", (D, 1024))
    xs_out = L.dram_out("xs", (B, 1024, S))
    zs_out = L.dram_out("zs", (B, 1024, S))
    wst = L.sb("wst", (128, 1024), F32)
    wxb = [L.sb(f"wxb{i}", (128, 1024), BF16) for i in range(32)]
    wzb = [L.sb(f"wzb{i}", (128, 1024), BF16) for i in range(32)]
    for i in range(32):
        L.ser.dma(wst[:, :], wx[i * 128:(i + 1) * 128, :])
        L.ser.op("scalar", lambda e, i=i: e.activation(out=wxb[i][:, :], in_=wst[:, :], func=AF.Copy))
        L.ser.dma(wst[:, :], wz[i * 128:(i + 1) * 128, :])
        L.ser.op("scalar", lambda e, i=i: e.activation(out=wzb[i][:, :], in_=wst[:, :], func=AF.Copy))
    st2 = L.sb("st2", (128, 512), F32)
    xbf = [L.sb(f"xbf{i}", (128, 512), BF16) for i in range(32)]
    p = L.pl(0)
    ev = L.sb("ev", (128, 512), F32)
    for b in range(B):
        for lc in range(8):
            csl = slice(lc * 512, (lc + 1) * 512)
            for i in range(32):
                L.ser.dma(st2[:, :], x2[b, i * 128:(i + 1) * 128, csl])
                L.ser.op("scalar", lambda e, i=i: e.activation(out=xbf[i][:, :], in_=st2[:, :], func=AF.Copy))
            for part in range(2):
                wsel = wxb if part == 0 else wzb
                osel = xs_out if part == 0 else zs_out
                for jt in range(8):
                    for i in range(32):
                        L.ser.op("tensor", _mm(p[:, :], wsel[i][:, jt * 128:(jt + 1) * 128], xbf[i][:, :], i == 0, i == 31))
                    L.ser.op("scalar", lambda e: e.activation(out=ev[:, :], in_=p[:, :], func=AF.Copy))
                    L.ser.dma(osel[b, jt * 128:(jt + 1) * 128, csl], ev[:, :], q="gpsimd")
    return L.finish()


def build_l1_conv():
    L = LBuilder("conv")
    xs = L.dram_in("xs", (B, 1024, S))
    cw = L.dram_in("cw", (1024, 4))
    cb = L.dram_in("cb", (8, 128))
    xc_out = L.dram_out("xc", (B, 1024, S))
    L.consts(have_bf_ident=False)
    cws = L.sb("cws", (128, 4), F32)
    cbc = L.col_from_1d("cbc", cb, 8, 128)
    xt = L.sb("xt", (128, S), F32)
    acc = L.sb("acc", (128, S), F32)
    tmp = L.sb("tmp", (128, S), F32)
    for b in range(B):
        for dt in range(8):
            L.ser.dma(xt[:, :], xs[b, dt * 128:(dt + 1) * 128, :])
            L.ser.dma(cws[:, :], cw[dt * 128:(dt + 1) * 128, :])
            L.ser.op("vector", lambda e, dt=dt: e.tensor_scalar(out=acc[:, :], in0=xt[:, :], scalar1=cws[:, 3:4], scalar2=cbc[:, dt:dt + 1], op0=OP.mult, op1=OP.add))
            for k in range(3):
                sh = 3 - k
                L.ser.op("vector", lambda e, k=k, sh=sh: e.tensor_scalar(out=tmp[:, 0:S - sh], in0=xt[:, 0:S - sh], scalar1=cws[:, k:k + 1], scalar2=None, op0=OP.mult))
                L.ser.op("vector", lambda e, sh=sh: e.tensor_add(acc[:, sh:S], acc[:, sh:S], tmp[:, 0:S - sh]))
            L.ser.op("scalar", lambda e: e.activation(out=acc[:, :], in_=acc[:, :], func=AF.Silu))
            L.ser.dma(xc_out[b, dt * 128:(dt + 1) * 128, :], acc[:, :], q="gpsimd")
    return L.finish()


def build_l1_xproj():
    L = LBuilder("xproj")
    xc = L.dram_in("xc", (8192, TPC))
    xpwT = L.dram_in("xpwT", (8192, 288))
    dtpwT = L.dram_in("dtpwT", (256, 8192))
    dtb = L.dram_in("dtb", (64, 128))
    dtln = L.dram_in("dtln", (2, 128))
    bcln = L.dram_in("bcln", (1, 32))
    seg = L.dram_in("seg", (32, 2))
    seg2 = L.dram_in("seg2", (2, 32))
    dt_out = L.dram_out("dt", (8192, TPC))
    bc_out = L.dram_out("bc", (32, TPC))
    L.consts()
    EPSR = 1e-6
    xpb = [L.sb(f"xpb{d}", (128, 288), BF16) for d in range(64)]
    st = L.sb("st", (128, 512), F32)
    for d in range(64):
        L.ser.dma(st[:, 0:288], xpwT[d * 128:(d + 1) * 128, :])
        L.ser.op("scalar", lambda e, d=d: e.activation(out=xpb[d][:, :], in_=st[:, 0:288], func=AF.Copy))
    dpb = [[L.sb(f"dpb{r}_{d}", (128, 128), BF16) for d in range(64)] for r in range(2)]
    for r in range(2):
        for d in range(64):
            L.ser.dma(st[:, 0:128], dtpwT[r * 128:(r + 1) * 128, d * 128:(d + 1) * 128])
            L.ser.op("scalar", lambda e, r=r, d=d: e.activation(out=dpb[r][d][:, :], in_=st[:, 0:128], func=AF.Copy))
    dtbc = L.col_from_1d("dtbc", dtb, 64, 128)
    dtlnc = L.col_from_1d("dtlnc", dtln, 2, 128)
    bclnc = L.col_from_1d("bclnc", bcln, 1, 32)
    segs = L.sb("segs", (32, 2), BF16)
    L.ser.dma(st[0:32, 0:2], seg)
    L.ser.op("scalar", lambda e: e.activation(out=segs[:, :], in_=st[0:32, 0:2], func=AF.Copy))
    seg2s = L.sb("seg2s", (2, 32), F32)
    L.ser.dma(seg2s[:, :], seg2)
    pe0 = L.pl(0); pe1 = L.pl(1); pe2 = L.pl(2)
    e0 = L.sb("e0", (128, 512), F32); e1 = L.sb("e1", (128, 512), F32); e2 = L.sb("e2", (32, 512), F32)
    sq0 = L.sb("sq0", (128, 512), BF16); sq1 = L.sb("sq1", (128, 512), BF16); sq2 = L.sb("sq2", (32, 512), BF16)
    prow = L.pl(3)
    prow2 = L.pl(4)
    rows = L.sb("rows", (2, 512), F32)
    rbc = L.sb("rbc", (2, 512), F32)
    e0n = L.sb("e0n", (128, 512), BF16); e1n = L.sb("e1n", (128, 512), BF16)
    bcn = L.sb("bcn", (32, 512), F32)
    tmpf = L.sb("tmpf", (128, 512), F32)
    xcb = L.sb("xcb", (128, 512), BF16)
    pdt = L.pl(5)
    dts = L.sb("dts", (128, 512), F32)
    ab = L.sb("ab", (128, 512), F32)
    for tc in range(2):
        csl = slice(tc * 512, (tc + 1) * 512)
        for d in range(64):
            L.ser.dma(st[:, :], xc[d * 128:(d + 1) * 128, csl])
            L.ser.op("scalar", lambda e: e.activation(out=xcb[:, :], in_=st[:, :], func=AF.Copy))
            L.ser.op("tensor", _mm(pe0[:, :], xpb[d][:, 0:128], xcb[:, :], d == 0, d == 63))
            L.ser.op("tensor", _mm(pe1[:, :], xpb[d][:, 128:256], xcb[:, :], d == 0, d == 63))
            L.ser.op("tensor", _mm(pe2[0:32, :], xpb[d][:, 256:288], xcb[:, :], d == 0, d == 63))
        L.ser.op("scalar", lambda e: e.activation(out=e0[:, :], in_=pe0[:, :], func=AF.Copy))
        L.ser.op("scalar", lambda e: e.activation(out=e1[:, :], in_=pe1[:, :], func=AF.Copy))
        L.ser.op("scalar", lambda e: e.activation(out=e2[:, :], in_=pe2[0:32, :], func=AF.Copy))
        L.ser.op("scalar", lambda e: e.activation(out=sq0[:, :], in_=e0[:, :], func=AF.Square))
        L.ser.op("scalar", lambda e: e.activation(out=sq1[:, :], in_=e1[:, :], func=AF.Square))
        L.ser.op("scalar", lambda e: e.activation(out=sq2[:, :], in_=e2[:, :], func=AF.Square))
        L.ser.op("tensor", _mm(prow[0:1, :], L.ones_col_b[:, :], sq0[:, :], True, False))
        L.ser.op("tensor", _mm(prow[0:1, :], L.ones_col_b[:, :], sq1[:, :], False, True))
        L.ser.op("tensor", _mm(prow2[0:2, :], segs[:, :], sq2[:, :], True, True))
        L.ser.op("scalar", lambda e: e.activation(out=rows[0:1, :], in_=prow[0:1, :], func=AF.Ln, scale=1.0 / 256.0, bias=EPSR))
        L.ser.op("scalar", lambda e: e.activation(out=rows[0:1, :], in_=rows[0:1, :], func=AF.Exp, scale=-0.5))
        L.ser.op("scalar", lambda e: e.activation(out=rbc[0:2, :], in_=prow2[0:2, :], func=AF.Ln, scale=1.0 / 16.0, bias=EPSR))
        L.ser.op("scalar", lambda e: e.activation(out=rbc[0:2, :], in_=rbc[0:2, :], func=AF.Exp, scale=-0.5))
        L.ser.op("tensor", _mm(prow[:, :], L.ones_row_f[:, :], rows[0:1, :], True, True))
        L.ser.op("vector", lambda e: e.tensor_tensor(out=tmpf[:, :], in0=e0[:, :], in1=prow[:, :], op=OP.mult))
        L.ser.op("vector", lambda e: e.tensor_scalar(out=e0n[:, :], in0=tmpf[:, :], scalar1=dtlnc[:, 0:1], scalar2=None, op0=OP.mult))
        L.ser.op("vector", lambda e: e.tensor_tensor(out=tmpf[:, :], in0=e1[:, :], in1=prow[:, :], op=OP.mult))
        L.ser.op("vector", lambda e: e.tensor_scalar(out=e1n[:, :], in0=tmpf[:, :], scalar1=dtlnc[:, 1:2], scalar2=None, op0=OP.mult))
        L.ser.op("tensor", _mm(prow2[0:32, :], seg2s[:, :], rbc[0:2, :], True, True))
        L.ser.op("vector", lambda e: e.tensor_tensor(out=bcn[:, :], in0=e2[:, :], in1=prow2[0:32, :], op=OP.mult))
        L.ser.op("vector", lambda e: e.tensor_scalar(out=bcn[:, :], in0=bcn[:, :], scalar1=bclnc[0:32, 0:1], scalar2=None, op0=OP.mult))
        L.ser.dma(bc_out[:, csl], bcn[:, :], q="gpsimd")
        for d in range(64):
            L.ser.op("tensor", _mm(pdt[:, :], dpb[0][d][:, :], e0n[:, :], True, False))
            L.ser.op("tensor", _mm(pdt[:, :], dpb[1][d][:, :], e1n[:, :], False, True))
            L.ser.op("vector", lambda e, d=d: e.tensor_scalar(out=dts[:, :], in0=pdt[:, :], scalar1=dtbc[:, d:d + 1], scalar2=None, op0=OP.add))
            L.ser.op("scalar", lambda e: e.activation(out=ab[:, :], in_=dts[:, :], func=AF.Abs))
            L.ser.op("scalar", lambda e: e.activation(out=ab[:, :], in_=ab[:, :], func=AF.Exp, scale=-1.0))
            L.ser.op("scalar", lambda e: e.activation(out=ab[:, :], in_=ab[:, :], func=AF.Ln, bias=1.0))
            L.ser.op("vector", lambda e: e.tensor_relu(out=dts[:, :], in_=dts[:, :]))
            L.ser.op("vector", lambda e: e.tensor_add(dts[:, :], dts[:, :], ab[:, :]))
            L.ser.dma(dt_out[d * 128:(d + 1) * 128, csl], dts[:, :], q="gpsimd")
    return L.finish()


def build_l1_scan():
    L = LBuilder("scan")
    dt_in = L.dram_in("dt", (B, 1024, S))
    xc_in = L.dram_in("xc", (B, 1024, S))
    z_in = L.dram_in("z", (B, 1024, S))
    bc_in = L.dram_in("bc", (B, 32, S))
    alog = L.dram_in("alog", (1024, 16))
    dpar = L.dram_in("dpar", (8, 128))
    y_out = L.dram_out("y", (B, 1024, S), BF16)
    L.consts(have_bf_ident=False)
    acols = [L.sb(f"ac{d}", (128, 16), F32) for d in range(8)]
    for d in range(8):
        L.ser.dma(acols[d][:, :], alog[d * 128:(d + 1) * 128, :])
        L.ser.op("scalar", lambda e, d=d: e.activation(out=acols[d][:, :], in_=acols[d][:, :], func=AF.Exp))
        L.ser.op("vector", lambda e, d=d: e.tensor_single_scalar(out=acols[d][:, :], in_=acols[d][:, :], scalar=-1.0, op=OP.mult))
    dcols = L.col_from_1d("dcols", dpar, 8, 128)
    carry = L.sb("carry", (128, 128), F32)
    LC = 1024
    dtt = [L.sb(f"dtt{d}", (128, LC), F32) for d in range(8)]
    xct = [L.sb(f"xct{d}", (128, LC), F32) for d in range(8)]
    dtx = [L.sb(f"dtx{d}", (128, LC), F32) for d in range(8)]
    ya = [L.sb(f"ya{d}", (128, LC), F32) for d in range(8)]
    brow = L.sb("brow", (1, LC), F32)
    crow = L.sb("crow", (1, LC), F32)
    bbc = L.sb("bbc", (128, LC), F32)
    cbc = L.sb("cbc", (128, LC), F32)
    prep = L.pl(0)
    dA = L.sb("dA", (128, LC), F32)
    uu = L.sb("uu", (128, LC), F32)
    hh = L.sb("hh", (128, LC), F32)
    tmp = L.sb("tmp", (128, LC), F32)
    zt = L.sb("zt", (128, LC), F32)
    ybf = L.sb("ybf2", (128, LC), BF16)
    for b in range(B):
        L.ser.op("vector", lambda e: e.memset(carry[:, :], 0.0))
        for lc in range(4):
            lsl = slice(lc * LC, (lc + 1) * LC)
            for d in range(8):
                L.ser.dma(dtt[d][:, :], dt_in[b, d * 128:(d + 1) * 128, lsl])
                L.ser.dma(xct[d][:, :], xc_in[b, d * 128:(d + 1) * 128, lsl])
                L.ser.op("vector", lambda e, d=d: e.tensor_mul(dtx[d][:, :], dtt[d][:, :], xct[d][:, :]))
            for n in range(16):
                L.ser.dma(brow[0:1, :], bc_in[b, n:n + 1, lsl])
                L.ser.dma(crow[0:1, :], bc_in[b, 16 + n:17 + n, lsl])
                for half in range(2):
                    hs = slice(half * 512, (half + 1) * 512)
                    L.ser.op("tensor", _mm(prep[:, :], L.ones_row_f[:, :], brow[0:1, hs], True, True))
                    L.ser.op("scalar", lambda e, hs=hs: e.activation(out=bbc[:, hs], in_=prep[:, :], func=AF.Copy))
                    L.ser.op("tensor", _mm(prep[:, :], L.ones_row_f[:, :], crow[0:1, hs], True, True))
                    L.ser.op("scalar", lambda e, hs=hs: e.activation(out=cbc[:, hs], in_=prep[:, :], func=AF.Copy))
                for d in range(8):
                    col = d * 16 + n
                    L.ser.op("scalar", lambda e, d=d, n=n: e.activation(out=dA[:, :], in_=dtt[d][:, :], func=AF.Exp, scale=acols[d][:, n:n + 1]))
                    L.ser.op("vector", lambda e, d=d: e.tensor_mul(uu[:, :], dtx[d][:, :], bbc[:, :]))
                    L.ser.op("vector", lambda e, col=col: e.tensor_tensor_scan(out=hh[:, :], data0=dA[:, :], data1=uu[:, :], initial=carry[:, col:col + 1], op0=OP.mult, op1=OP.add))
                    L.ser.op("vector", lambda e, col=col: e.tensor_copy(out=carry[:, col:col + 1], in_=hh[:, LC - 1:LC]))
                    if n == 0:
                        L.ser.op("vector", lambda e, d=d: e.tensor_mul(ya[d][:, :], hh[:, :], cbc[:, :]))
                    else:
                        L.ser.op("vector", lambda e: e.tensor_mul(tmp[:, :], hh[:, :], cbc[:, :]))
                        L.ser.op("vector", lambda e, d=d: e.tensor_add(ya[d][:, :], ya[d][:, :], tmp[:, :]))
            for d in range(8):
                L.ser.dma(zt[:, :], z_in[b, d * 128:(d + 1) * 128, lsl])
                L.ser.op("scalar", lambda e: e.activation(out=zt[:, :], in_=zt[:, :], func=AF.Silu))
                L.ser.op("vector", lambda e, d=d: e.tensor_scalar(out=tmp[:, :], in0=xct[d][:, :], scalar1=dcols[:, d:d + 1], scalar2=None, op0=OP.mult))
                L.ser.op("vector", lambda e, d=d: e.tensor_add(tmp[:, :], tmp[:, :], ya[d][:, :]))
                L.ser.op("vector", lambda e: e.tensor_mul(ybf[:, :], tmp[:, :], zt[:, :]))
                L.ser.dma(y_out[b, d * 128:(d + 1) * 128, lsl], ybf[:, :], q="gpsimd")
    return L.finish()


def build_l1_outproj():
    L = LBuilder("outp")
    y_in = L.dram_in("y", (8192, TPC), BF16)
    woT = L.dram_in("woT", (8192, D))
    o_out = L.dram_out("o", (D, TPC), BF16)
    rhs = [L.sb(f"rhs{d}", (128, TPC), BF16) for d in range(64)]
    for d in range(64):
        L.ser.dma(rhs[d][:, :], y_in[d * 128:(d + 1) * 128, :])
    wst = L.sb("wst", (128, 512), F32)
    wbf = L.sb("wbf", (128, 512), BF16)
    pss = [L.pl(i) for i in range(8)]
    ev = L.sb("ev", (128, TPC), BF16)
    for og in range(8):
        for d in range(64):
            L.ser.dma(wst[:, :], woT[d * 128:(d + 1) * 128, og * 512:(og + 1) * 512])
            L.ser.op("scalar", lambda e: e.activation(out=wbf[:, :], in_=wst[:, :], func=AF.Copy))
            for ot in range(4):
                for tc in range(2):
                    L.ser.op("tensor", _mm(pss[ot * 2 + tc][:, :], wbf[:, ot * 128:(ot + 1) * 128],
                                           rhs[d][:, tc * 512:(tc + 1) * 512], d == 0, d == 63))
        for ot in range(4):
            for tc in range(2):
                L.ser.op("scalar", lambda e, ot=ot, tc=tc: e.activation(out=ev[:, tc * 512:(tc + 1) * 512], in_=pss[ot * 2 + tc][:, :], func=AF.Copy))
            L.ser.dma(o_out[(og * 4 + ot) * 128:(og * 4 + ot + 1) * 128, :], ev[:, :], q="gpsimd")
    return L.finish()


def build_resln():
    L = LBuilder("resln")
    y_in = L.dram_in("yb", (D, TPC), BF16)
    xres = L.dram_in("xres", (D, TPC))
    out = L.dram_out("out", (D, TPC))
    L.consts(have_bf_ident=False)
    ybf = [L.sb(f"ybf{o}", (128, TPC), BF16) for o in range(32)]
    for o in range(32):
        L.ser.dma(ybf[o][:, :], y_in[o * 128:(o + 1) * 128, :])
    L.ln_dmajor(
        [ybf[o][:, :] for o in range(32)],
        [xres[o * 128:(o + 1) * 128, :] for o in range(32)],
        [out[o * 128:(o + 1) * 128, :] for o in range(32)], TPC)
    return L.finish()


def _run(nc, in_maps):
    return run_bass_kernel_spmd(nc, in_maps, core_ids=list(range(NCORE))).results


def kernel(**inp):
    f32 = np.float32
    x0 = np.ascontiguousarray(inp["L0_attn_hidden_states"].astype(f32, copy=False))
    x0T = np.ascontiguousarray(x0.transpose(0, 2, 1))

    ident_f = np.eye(128, dtype=f32)
    ident_b = np.eye(128, dtype=BFNP)
    mask = np.where(np.arange(128)[None, :] > np.arange(128)[:, None], f32(NEG), f32(0.0)).astype(f32)
    inv_freq = (1.0 / (10000.0 ** (np.arange(0, 128, 2).astype(np.float64) / 128.0))).astype(np.float64)
    t = np.arange(S, dtype=np.float64)
    fr = np.outer(t, inv_freq)
    emb = np.concatenate([fr, fr], axis=1)
    cosT = np.ascontiguousarray(np.cos(emb).T.astype(f32))
    sinT = np.ascontiguousarray(np.sin(emb).T.astype(f32))

    wqT = np.ascontiguousarray(inp["L0_attn_Wq"].T.astype(f32))
    wkT = np.ascontiguousarray(inp["L0_attn_Wk"].T.astype(f32))
    wvT = np.ascontiguousarray(inp["L0_attn_Wv"].T.astype(f32))
    nc = build_l0_qkv()
    maps = []
    for c in range(NCORE):
        b, s0 = c // 4, (c % 4) * TPC
        maps.append(dict(xT=np.ascontiguousarray(x0T[b, :, s0:s0 + TPC]), wqT=wqT, wkT=wkT, wvT=wvT,
                         cosT=np.ascontiguousarray(cosT[:, s0:s0 + TPC]),
                         sinT=np.ascontiguousarray(sinT[:, s0:s0 + TPC]),
                         ident_f32=ident_f, ident_bf16=ident_b))
    r = _run(nc, maps)
    QK = np.empty((B, 40, 128, S), dtype=BFNP)
    V = np.empty((B, 8, S, 128), dtype=BFNP)
    for c in range(NCORE):
        b, s0 = c // 4, (c % 4) * TPC
        QK[b, :, :, s0:s0 + TPC] = r[c]["qk"]
        V[b, :, s0:s0 + TPC, :] = r[c]["v"]

    nc = build_l0_attn()
    maps = []
    for c in range(NCORE):
        b, h0 = c // 4, (c % 4) * 8
        kv0 = h0 // 4
        maps.append(dict(qt=np.ascontiguousarray(QK[b, h0:h0 + 8]),
                         kt=np.ascontiguousarray(QK[b, 32 + kv0:32 + kv0 + 2]),
                         v=np.ascontiguousarray(V[b, kv0:kv0 + 2]),
                         mask=mask, ident_f32=ident_f, ident_bf16=ident_b))
    r = _run(nc, maps)
    AO = np.empty((B, 32, 128, S), dtype=BFNP)
    for c in range(NCORE):
        b, h0 = c // 4, (c % 4) * 8
        AO[b, h0:h0 + 8] = r[c]["ao"]

    woT = np.ascontiguousarray(inp["L0_attn_Wo"].T.astype(f32))
    nc = build_l0_wo_ln()
    maps = []
    for c in range(NCORE):
        b, s0 = c // 4, (c % 4) * TPC
        maps.append(dict(ao=np.ascontiguousarray(AO[b, :, :, s0:s0 + TPC].reshape(D, TPC)),
                         woT=woT, xres=np.ascontiguousarray(x0T[b, :, s0:s0 + TPC]),
                         ident_f32=ident_f, ident_bf16=ident_b))
    r = _run(nc, maps)
    X1 = np.empty((B, D, S), dtype=f32)
    for c in range(NCORE):
        b, s0 = c // 4, (c % 4) * TPC
        X1[b, :, s0:s0 + TPC] = r[c]["x1"]

    def run_ffn(Xin, mode, wbT, wrT, aT, bfl, bb, repscale, brv):
        nc = build_ffn(mode)
        rep = np.zeros((8, 128), dtype=f32)
        for e in range(8):
            rep[e, e * 16:(e + 1) * 16] = repscale
        maps = []
        for c in range(NCORE):
            b, s0 = c // 4, (c % 4) * TPC
            maps.append(dict(xT=np.ascontiguousarray(Xin[b, :, s0:s0 + TPC]), wbT=wbT, wrT=wrT,
                             aT=aT, bfl=bfl, bb=bb, rep=rep, br=brv,
                             ident_f32=ident_f, ident_bf16=ident_b))
        r = _run(nc, maps)
        Xo = np.empty((B, D, S), dtype=f32)
        for c in range(NCORE):
            b, s0 = c // 4, (c % 4) * TPC
            Xo[b, :, s0:s0 + TPC] = r[c]["out"]
        return Xo

    X2 = run_ffn(
        X1, "top2",
        np.ascontiguousarray(inp["L0_ffn_W_base"].T.astype(f32)),
        np.ascontiguousarray(inp["L0_ffn_W_router"].T.astype(f32)),
        np.ascontiguousarray(inp["L0_ffn_A"].reshape(128, D).T),
        np.ascontiguousarray(inp["L0_ffn_B"].transpose(0, 2, 1).reshape(128, D)),
        np.ascontiguousarray(inp["L0_ffn_b_base"].reshape(32, 128)),
        2.0, np.zeros((1, 8), dtype=f32))

    wip = inp["L1_attn_in_proj_w"].astype(f32, copy=False)
    nc = build_l1_inproj()
    maps = []
    for c in range(NCORE):
        maps.append(dict(x2=X2,
                         wx=np.ascontiguousarray(wip[:, c * 1024:(c + 1) * 1024]),
                         wz=np.ascontiguousarray(wip[:, 8192 + c * 1024:8192 + (c + 1) * 1024])))
    r = _run(nc, maps)
    XS = np.empty((B, 8192, S), dtype=f32)
    ZS = np.empty((B, 8192, S), dtype=f32)
    for c in range(NCORE):
        XS[:, c * 1024:(c + 1) * 1024, :] = r[c]["xs"]
        ZS[:, c * 1024:(c + 1) * 1024, :] = r[c]["zs"]

    cw = inp["L1_attn_conv_w"].astype(f32, copy=False)
    cb = inp["L1_attn_conv_b"].astype(f32, copy=False)
    nc = build_l1_conv()
    maps = []
    for c in range(NCORE):
        maps.append(dict(xs=np.ascontiguousarray(XS[:, c * 1024:(c + 1) * 1024, :]),
                         cw=np.ascontiguousarray(cw[c * 1024:(c + 1) * 1024, :]),
                         cb=np.ascontiguousarray(cb[c * 1024:(c + 1) * 1024].reshape(8, 128)),
                         ident_f32=ident_f))
    r = _run(nc, maps)
    XC = np.empty((B, 8192, S), dtype=f32)
    for c in range(NCORE):
        XC[:, c * 1024:(c + 1) * 1024, :] = r[c]["xc"]

    xpwT = np.ascontiguousarray(inp["L1_attn_x_proj_w"].T.astype(f32))
    dtpwT = np.ascontiguousarray(inp["L1_attn_dt_proj_w"].T.astype(f32))
    seg = np.zeros((32, 2), dtype=f32)
    seg[0:16, 0] = 1.0
    seg[16:32, 1] = 1.0
    seg2 = np.ascontiguousarray(seg.T)
    bcln = np.concatenate([inp["L1_attn_B_ln_w"], inp["L1_attn_C_ln_w"]]).astype(f32).reshape(1, 32)
    nc = build_l1_xproj()
    maps = []
    for c in range(NCORE):
        b, l0 = c // 4, (c % 4) * TPC
        maps.append(dict(xc=np.ascontiguousarray(XC[b, :, l0:l0 + TPC]), xpwT=xpwT, dtpwT=dtpwT,
                         dtb=np.ascontiguousarray(inp["L1_attn_dt_bias"].astype(f32).reshape(64, 128)),
                         dtln=np.ascontiguousarray(inp["L1_attn_dt_ln_w"].astype(f32).reshape(2, 128)),
                         bcln=bcln, seg=seg, seg2=seg2,
                         ident_f32=ident_f, ident_bf16=ident_b))
    r = _run(nc, maps)
    DT = np.empty((B, 8192, S), dtype=f32)
    BC = np.empty((B, 32, S), dtype=f32)
    for c in range(NCORE):
        b, l0 = c // 4, (c % 4) * TPC
        DT[b, :, l0:l0 + TPC] = r[c]["dt"]
        BC[b, :, l0:l0 + TPC] = r[c]["bc"]

    alog = inp["L1_attn_A_log"].astype(f32, copy=False)
    dpar = inp["L1_attn_D_param"].astype(f32, copy=False)
    nc = build_l1_scan()
    maps = []
    for c in range(NCORE):
        sl = slice(c * 1024, (c + 1) * 1024)
        maps.append(dict(dt=np.ascontiguousarray(DT[:, sl, :]), xc=np.ascontiguousarray(XC[:, sl, :]),
                         z=np.ascontiguousarray(ZS[:, sl, :]), bc=BC,
                         alog=np.ascontiguousarray(alog[sl, :]),
                         dpar=np.ascontiguousarray(dpar[sl].reshape(8, 128)),
                         ident_f32=ident_f))
    r = _run(nc, maps)
    Y = np.empty((B, 8192, S), dtype=BFNP)
    for c in range(NCORE):
        Y[:, c * 1024:(c + 1) * 1024, :] = r[c]["y"]

    woutT = np.ascontiguousarray(inp["L1_attn_out_proj_w"].T.astype(f32))
    nc = build_l1_outproj()
    maps = []
    for c in range(NCORE):
        b, s0 = c // 4, (c % 4) * TPC
        maps.append(dict(y=np.ascontiguousarray(Y[b, :, s0:s0 + TPC]), woT=woutT))
    r = _run(nc, maps)
    O9 = np.empty((B, D, S), dtype=BFNP)
    for c in range(NCORE):
        b, s0 = c // 4, (c % 4) * TPC
        O9[b, :, s0:s0 + TPC] = r[c]["o"]

    nc = build_resln()
    maps = []
    for c in range(NCORE):
        b, s0 = c // 4, (c % 4) * TPC
        maps.append(dict(yb=np.ascontiguousarray(O9[b, :, s0:s0 + TPC]),
                         xres=np.ascontiguousarray(X2[b, :, s0:s0 + TPC]),
                         ident_f32=ident_f))
    r = _run(nc, maps)
    X3 = np.empty((B, D, S), dtype=f32)
    for c in range(NCORE):
        b, s0 = c // 4, (c % 4) * TPC
        X3[b, :, s0:s0 + TPC] = r[c]["out"]

    X4 = run_ffn(
        X3, "sparsemax",
        np.ascontiguousarray(inp["L1_ffn_W_base"].T.astype(f32)),
        np.ascontiguousarray(inp["L1_ffn_W_router"].T.astype(f32)),
        np.ascontiguousarray(inp["L1_ffn_lora_A"].transpose(1, 0, 2).reshape(D, 128)),
        np.ascontiguousarray(inp["L1_ffn_lora_B"].reshape(128, D)),
        np.ascontiguousarray(inp["L1_ffn_b_base"].astype(f32).reshape(32, 128)),
        1.0, np.ascontiguousarray(inp["L1_ffn_b_router"].astype(f32).reshape(1, 8)))

    return np.ascontiguousarray(X4.transpose(0, 2, 1)).astype(np.float32)
